# version 32
# baseline (speedup 1.0000x reference)
import numpy as np
import ml_dtypes
import concourse.bass as bass
import concourse.mybir as mybir
from concourse.bass_utils import run_bass_kernel_spmd
from contextlib import ExitStack

F32 = mybir.dt.float32
BF16 = mybir.dt.bfloat16
I32 = mybir.dt.int32
AF = mybir.ActivationFunctionType
ALU = mybir.AluOpType
bfloat16 = ml_dtypes.bfloat16

D = 1024
H = 4096
SEQ = 2048
BPC = 2
NTOK = BPC * SEQ
NTILE = 32
TFF = 512
NFT = NTOK // TFF
GRP = TFF // 128
EPS = 1e-5
MAGIC = float(0x5F3759DF)
SCAN_EVERY = 9
DEBUG = False


class S:
    def __init__(self, stack, nc, name):
        self.h = stack.enter_context(nc.semaphore(name))
        self.n = 0
    def tok(self, k=1):
        self.n += k
        return self.n


def build_program():
    nc = bass.Bass("TRN2", target_bir_lowering=False, debug=False)

    xin = nc.dram_tensor("xin", [BPC, SEQ, D], F32, kind="ExternalInput").ap()
    w12p = nc.dram_tensor("w12p", [16, 128, 4096], F32, kind="ExternalInput").ap()
    h1p = nc.dram_tensor("h1p", [32, 128, 32 * 128], F32, kind="ExternalInput").ap()
    h2p = nc.dram_tensor("h2p", [32, 128, 32 * 128], F32, kind="ExternalInput").ap()
    w3p = nc.dram_tensor("w3p", [2, 4, 128, 8 * 512], F32, kind="ExternalInput").ap()
    btp = nc.dram_tensor("btp", [128, 8 * 128], F32, kind="ExternalInput").ap()
    ctp = nc.dram_tensor("ctp", [128, D], F32, kind="ExternalInput").ap()
    atp = nc.dram_tensor("atp", [128, 128], F32, kind="ExternalInput").ap()
    identp = nc.dram_tensor("identp", [128, 128], BF16, kind="ExternalInput").ap()
    gammap = nc.dram_tensor("gammap", [128, D], F32, kind="ExternalInput").ap()
    betap = nc.dram_tensor("betap", [128, D], F32, kind="ExternalInput").ap()
    dp = nc.dram_tensor("dp", [128, D], F32, kind="ExternalInput").ap()
    s3p = nc.dram_tensor("s3p", [128, D], F32, kind="ExternalInput").ap()
    s1p = nc.dram_tensor("s1p", [128, 32], F32, kind="ExternalInput").ap()
    s2p = nc.dram_tensor("s2p", [128, 32], F32, kind="ExternalInput").ap()
    h1sp = nc.dram_tensor("h1sp", [128, 32], F32, kind="ExternalInput").ap()
    h2sp = nc.dram_tensor("h2sp", [128, 32], F32, kind="ExternalInput").ap()
    out_d = nc.dram_tensor("out", [BPC, SEQ, D], F32, kind="ExternalOutput").ap()
    SCR = "ExternalOutput" if DEBUG else "Internal"
    wb12s = nc.dram_tensor("wb12s", [16, 128, 4096], BF16, kind=SCR).ap()
    wh1s = nc.dram_tensor("wh1s", [32, 128, 32 * 128], BF16, kind=SCR).ap()
    wh2s = nc.dram_tensor("wh2s", [32, 128, 32 * 128], BF16, kind=SCR).ap()
    w3s = nc.dram_tensor("w3s", [2, 4, 128, 8 * 512], BF16, kind=SCR).ap()
    if DEBUG:
        dbg_st = nc.dram_tensor("dbg_st", [128, NTOK], BF16, kind="ExternalOutput").ap()
        dbg_mu = nc.dram_tensor("dbg_mu", [128, 32], F32, kind="ExternalOutput").ap()
        dbg_r = nc.dram_tensor("dbg_r", [128, 32], F32, kind="ExternalOutput").ap()
        dbg_a = nc.dram_tensor("dbg_a", [128, 32 * 512], BF16, kind="ExternalOutput").ap()
        dbg_g1 = nc.dram_tensor("dbg_g1", [128, 32 * 512], BF16, kind="ExternalOutput").ap()
        dbg_g2 = nc.dram_tensor("dbg_g2", [128, 32 * 512], BF16, kind="ExternalOutput").ap()

    with ExitStack() as stack:
        sb = lambda n_, s, d: stack.enter_context(nc.sbuf_tensor(n_, s, d))
        psa = lambda n_, s, d: stack.enter_context(nc.psum_tensor(n_, s, d))

        gammaB = sb("gammaB", [128, D], F32)
        betaB = sb("betaB", [128, D], F32)
        d1B = sb("d1B", [128, D], F32)
        s3B = sb("s3B", [128, D], F32)
        ctb = sb("ctb", [128, D], BF16)
        btb = sb("btb", [128, 8 * 128], BF16)
        atb = sb("atb", [128, 128], BF16)
        ident = sb("ident", [128, 128], BF16)
        s1c = sb("s1c", [128, 32], F32)
        s2c = sb("s2c", [128, 32], F32)
        h1sc = sb("h1sc", [128, 32], F32)
        h2sc = sb("h2sc", [128, 32], F32)
        xt = [sb(f"xt{i}", [128, D], F32) for i in range(2)]
        xa = [sb(f"xa{i}", [128, D], F32) for i in range(2)]
        xn_a = [sb(f"xn_a{i}", [128, D], BF16) for i in range(2)]
        xnT = [sb(f"xnT{i}", [128, 8 * 512], BF16) for i in range(2)]
        wsl = [sb(f"wsl{i}", [128, 4096], BF16) for i in range(4)]
        a_sb = sb("a_sb", [128, 32 * 512], BF16)
        g1_sb = sb("g1_sb", [128, 32 * 512], BF16)
        attn_sb = [g1_sb[:, 8192 + 2048 * m:8192 + 2048 * (m + 1)].bitcast(F32)
                   for m in range(4)]
        u_sb = [sb(f"u_sb{i}", [128, 512], F32) for i in range(2)]
        v_sb = [sb(f"v_sb{i}", [128, 512], F32) for i in range(2)]
        o3_sb = sb("o3_sb", [128, 4 * D], F32)
        cbA = sb("cbA", [128, D], F32)
        cbB = sb("cbB", [128, D], F32)
        cbC = sb("cbC", [128, D], F32)
        outs = [sb(f"outs{i}", [128, D], F32) for i in range(2)]
        xb2 = sb("xb2", [128, 2 * 512], F32)
        statesT = sb("statesT", [128, 2 * 512], BF16)
        mu_arr = sb("mu_arr", [128, 32], F32)
        r_arr = sb("r_arr", [128, 32], F32)
        mr_arr = sb("mr_arr", [128, 32], F32)
        bn6 = [sb(f"bn6_{i}", [128, 12], F32) for i in range(2)]
        mv = [sb(f"mv{i}", [128, 2], F32) for i in range(2)]
        nw = [sb(f"nw{i}", [128, 6], F32) for i in range(2)]
        nwi = [sb(f"nwi{i}", [128, 1], I32) for i in range(2)]
        bn6C = [sb(f"bn6C{i}", [128, 12], F32) for i in range(2)]
        mvC = [sb(f"mvC{i}", [128, 2], F32) for i in range(2)]
        nwC = [sb(f"nwC{i}", [128, 6], F32) for i in range(2)]
        nwiC = [sb(f"nwiC{i}", [128, 1], I32) for i in range(2)]
        r2r = [sb(f"r2r{i}", [128, 1], F32) for i in range(2)]
        mr2r = [sb(f"mr2r{i}", [128, 1], F32) for i in range(2)]
        tch = sb("tch", [128, 2], F32)

        P = [psa(f"P{i}", [128, 512], F32) for i in range(6)]
        TB = [psa(f"TB{i}", [128, 1024], BF16) for i in range(2)]

        s_cast = S(stack, nc, "s_cast")
        s_pc = S(stack, nc, "s_pc")
        s_cl = S(stack, nc, "s_cl")
        s_xt = [S(stack, nc, f"s_xt{i}") for i in range(2)]
        s_xa = [S(stack, nc, f"s_xa{i}") for i in range(2)]
        s_ws = [S(stack, nc, f"s_ws{i}") for i in range(4)]
        s_scws = [S(stack, nc, f"s_scw{i}") for i in range(4)]
        s_st = [S(stack, nc, f"s_st{i}") for i in range(2)]
        s_te = S(stack, nc, "s_te")
        s_act = S(stack, nc, "s_act")
        s_dve = S(stack, nc, "s_dve")
        s_plc = S(stack, nc, "s_plc")
        s_smm = S(stack, nc, "s_smm")
        s_sth = S(stack, nc, "s_sth")

        E = {k: [] for k in ("sp", "pool", "te", "act", "dve")}

        def emit(eng, fn):
            E[eng].append(fn)

        def wait(eng, sem, val):
            if val > 0:
                emit(eng, lambda e, s=sem.h, v=val: e.wait_ge(s, v))

        bank_free = {}

        def bank_wait(eng, bk):
            if bk in bank_free:
                sem, val = bank_free[bk]
                wait(eng, sem, val)

        def bank_set_free(bk, sem, val):
            bank_free[bk] = (sem, val)

        def wait_free(eng, entry):
            if entry is None:
                return
            if entry[0] == "both":
                _, a, b_ = entry
                wait(eng, a[0], a[1])
                wait(eng, b_[0], b_[1])
            else:
                wait(eng, entry[0], entry[1])

        def dma(eng, sem, out, in_):
            t = sem.tok(16)
            emit(eng, lambda e, o=out, i=in_, s=sem.h: e.dma_start(out=o, in_=i).then_inc(s, 16))
            return t

        def dve_step(fn, waits=()):
            for sem, val in waits:
                wait("dve", sem, val)
            tok = s_dve.tok()
            emit("dve", lambda e, f=fn: f(e).then_inc(s_dve.h, 1))
            return tok

        def pool_step(fn, waits=()):
            for sem, val in waits:
                wait("pool", sem, val)
            tok = s_plc.tok()
            emit("pool", lambda e, f=fn: f(e).then_inc(s_plc.h, 1))
            return tok

        t_pc = 0
        for dst, src in ((ctb, ctp), (btb, btp), (atb, atp), (ident, identp)):
            t_pc = dma("pool", s_pc, dst[:, :], src[:, :])

        def emit_sp_consts():
            t = 0
            for dst, src in ((gammaB, gammap), (betaB, betap), (s3B, s3p),
                             (s1c, s1p), (s2c, s2p),
                             (h1sc, h1sp), (h2sc, h2sp), (cbA, dp)):
                t = dma("sp", s_cl, dst[:, :], src[:, :])
            return t

        WSRC = {}
        WSCR = {}
        for jj in range(16):
            WSRC[("w12", jj)] = w12p[jj]
            WSCR[("w12", jj)] = wb12s[jj]
        for j in range(32):
            WSRC[("h1", j)] = h1p[j]
            WSCR[("h1", j)] = wh1s[j]
            WSRC[("h2", j)] = h2p[j]
            WSCR[("h2", j)] = wh2s[j]
        for half in range(2):
            for kg in range(4):
                WSRC[("w3", half, kg)] = w3p[half, kg]
                WSCR[("w3", half, kg)] = w3s[half, kg]
        scw_tok = {}
        slot_extra = [None, None, None, None]

        def newton_rsqrt(var_ap, out_ap, ring, wset=None):
            w, wi = wset if wset is not None else (nw[ring], nwi[ring])
            ve = w[:, 0:1]
            ta = w[:, 1:2]
            ys = [w[:, 2:3], w[:, 3:4]]
            yy = w[:, 4:5]
            seq = []
            seq.append(lambda e: e.tensor_scalar(ve, var_ap, EPS, None, op0=ALU.add))
            seq.append(lambda e: e.tensor_scalar(wi[:, :], ve.bitcast(I32), 1, None,
                                                 op0=ALU.logical_shift_right))
            seq.append(lambda e: e.tensor_scalar(ta, wi[:, :], -1.0, MAGIC,
                                                 op0=ALU.mult, op1=ALU.add))
            seq.append(lambda e: e.tensor_copy(wi[:, :], ta))
            seq.append(lambda e: e.tensor_copy(ys[0], wi[:, :].bitcast(F32)))
            for it in range(2):
                cur = ys[it % 2]
                dst = out_ap if it == 1 else ys[(it + 1) % 2]
                seq.append(lambda e, cur=cur: e.tensor_tensor(yy, cur, cur, op=ALU.mult))
                seq.append(lambda e: e.scalar_tensor_tensor(ta, yy, -0.5, ve,
                                                            op0=ALU.mult, op1=ALU.mult))
                seq.append(lambda e, d=dst, cur=cur: e.scalar_tensor_tensor(
                    d, ta, 1.5, cur, op0=ALU.add, op1=ALU.mult))
            last_tok = None
            for k, f in enumerate(seq):
                tok = s_dve.tok()
                emit("dve", lambda e, f=f: f(e).then_inc(s_dve.h, 1))
                if k < len(seq) - 1:
                    wait("dve", s_dve, tok)
                last_tok = tok
            return last_tok

        scan_next = [0]
        scan_limit = [0]
        mm_count = [0]
        xb_evac_tok = {}
        open_mm = {0: 0}
        scan_due_prev = [-SCAN_EVERY]

        def scan_due(t):
            w = t // 256
            return max(scan_due_prev[0] + SCAN_EVERY, open_mm[w] + 2)

        attn_last_tok = {}

        def st_col(t):
            w = t // 256
            kt_g = t // 64
            return 512 * (w % 2) + 128 * (kt_g % 4) + (t % 64)

        def emit_scan_step(t):
            kt_g = t // 64
            slot = (kt_g // 4) % 2
            c0s = 512 * slot + 128 * (kt_g % 4) + (t % 64)
            c0w = st_col(t)
            w = t // 256
            if t % 256 == 0 and w >= 2:
                wait("act", s_te, attn_last_tok[w - 2])
            if t == 0:
                emit("act", lambda e, c0s=c0s: e.activation(
                    statesT[:, 0:1], xb2[:, c0s:c0s + 1], AF.Tanh,
                    scale=0.0, bias=xb2[:, c0s:c0s + 1]))
                emit("act", lambda e, c0s=c0s: e.activation(
                    statesT[:, 64:65], xb2[:, c0s + 64:c0s + 65], AF.Tanh,
                    scale=0.0, bias=xb2[:, c0s + 64:c0s + 65]))
                s_sth.tok()
                emit("act", lambda e: e.copy(
                    tch[:, 0:2], statesT[:, 0:65:64]).then_inc(s_sth.h, 1))
                return
            p0w = st_col(t - 1)
            sl = t % 2
            wait("te", s_sth, t)
            t_mm = s_smm.tok()
            emit("te", lambda e, sl=sl, p0w=p0w: e.matmul(
                P[3][:, 2 * sl:2 * sl + 2], atb[:, :],
                statesT[:, p0w:p0w + 65:64], start=True, stop=True).then_inc(s_smm.h, 1))
            wait("act", s_smm, t_mm)
            emit("act", lambda e, sl=sl, c0w=c0w, c0s=c0s: e.activation(
                statesT[:, c0w:c0w + 1], P[3][:, 2 * sl:2 * sl + 1], AF.Tanh,
                bias=xb2[:, c0s:c0s + 1]))
            emit("act", lambda e, sl=sl, c0w=c0w, c0s=c0s: e.activation(
                statesT[:, c0w + 64:c0w + 65], P[3][:, 2 * sl + 1:2 * sl + 2], AF.Tanh,
                bias=xb2[:, c0s + 64:c0s + 65]))
            s_sth.tok()
            emit("act", lambda e, c0w=c0w: e.copy(
                tch[:, 0:2], statesT[:, c0w:c0w + 65:64]).then_inc(s_sth.h, 1))

        def pump_scan():
            while scan_next[0] < scan_limit[0]:
                d = scan_due(scan_next[0])
                if d > mm_count[0]:
                    break
                emit_scan_step(scan_next[0])
                scan_due_prev[0] = d
                scan_next[0] += 1

        def drain_scan(upto):
            while scan_next[0] < upto:
                emit_scan_step(scan_next[0])
                scan_next[0] += 1

        xt_free = [None, None]
        xna_free = [None, None]
        xnT_free = [None, None]
        group_xnT_tok = {}
        tp_evac_tok = {}
        pro_xna_tok = {}

        def emit_prologue_load(kt):
            sl = kt % 2
            wait_free("sp", xt_free[sl])
            dma("sp", s_xt[sl], xt[sl][0:64, :], xin[0, 64 * kt:64 * kt + 64, :])
            t_x = dma("sp", s_xt[sl], xt[sl][64:128, :], xin[1, 64 * kt:64 * kt + 64, :])
            wait("dve", s_xt[sl], t_x)
            for fn in [
                (lambda e, sl=sl: e.bn_stats(bn6[sl][:, 0:6], xt[sl][:, 0:512])),
                (lambda e, sl=sl: e.bn_stats(bn6[sl][:, 6:12], xt[sl][:, 512:1024])),
                (lambda e, sl=sl: e.bn_aggr(mv[sl][:, :], bn6[sl][:, :])),
            ]:
                tok = s_dve.tok()
                emit("dve", lambda e, f=fn: f(e).then_inc(s_dve.h, 1))
                wait("dve", s_dve, tok)
            newton_rsqrt(mv[sl][:, 1:2], r_arr[:, kt:kt + 1], sl)
            tok = s_dve.tok()
            emit("dve", lambda e, kt=kt, sl=sl: e.tensor_copy(
                mu_arr[:, kt:kt + 1], mv[sl][:, 0:1]).then_inc(s_dve.h, 1))
            wait("dve", s_dve, tok)
            wait_free("dve", xna_free[sl])
            tok = s_dve.tok()
            emit("dve", lambda e, kt=kt, sl=sl: e.tensor_scalar(
                xn_a[sl][:, :], xt[sl][:, :], mu_arr[:, kt:kt + 1], r_arr[:, kt:kt + 1],
                op0=ALU.subtract, op1=ALU.mult).then_inc(s_dve.h, 1))
            wait("dve", s_dve, tok)
            t_xna = s_dve.tok()
            emit("dve", lambda e, sl=sl: e.tensor_copy(
                tch[:, 0:1], xn_a[sl][:, 1023:1024]).then_inc(s_dve.h, 1))
            xt_free[sl] = (s_dve, t_xna)
            pro_xna_tok[kt] = t_xna

        def emit_prologue_te_chunk(kt, c):
            sl = kt % 2
            g = kt // GRP
            tloc = kt % GRP
            t_xna = pro_xna_tok[kt]
            if c == 0:
                wait("te", s_pc, t_pc)
                wait("te", s_dve, t_xna)
            t_tp = 0
            for k in range(2 * c, 2 * c + 2):
                bk = ("TB", k % 2)
                bank_wait("te", bk)
                t_tp = s_te.tok()
                emit("te", lambda e, sl=sl, k=k: e.transpose(
                    TB[k % 2][:, 0:128], xn_a[sl][:, 128 * k:128 * (k + 1)], ident[:, :]
                ).then_inc(s_te.h, 1))
                wait("act", s_te, t_tp)
                if tloc == 0 and k == 0:
                    wait_free("act", xnT_free[g % 2])
                t_ev = s_act.tok()
                emit("act", lambda e, g=g, tloc=tloc, k=k: e.copy(
                    xnT[g % 2][:, 512 * k + 128 * tloc:512 * k + 128 * tloc + 128],
                    TB[k % 2][:, 0:128]).then_inc(s_act.h, 1))
                bank_set_free(bk, s_act, t_ev)
            if c == 3:
                xna_free[sl] = (s_te, t_tp)
            if tloc == GRP - 1 and c == 3:
                tokg = s_act.tok()
                emit("act", lambda e, g=g: e.copy(
                    tch[:, 0:1], xnT[g % 2][:, 4094:4096].bitcast(F32)[:, 0:1]
                ).then_inc(s_act.h, 1))
                group_xnT_tok[g] = tokg
                bank_wait("te", 4)
                wait("te", s_act, tokg)
                for k in range(8):
                    last = (k == 7)
                    if last:
                        s_te.tok()
                    def mk(g=g, k=k, last=last):
                        def f(e):
                            ins = e.matmul(P[4][:, 0:512], btb[:, 128 * k:128 * (k + 1)],
                                           xnT[g % 2][:, 512 * k:512 * (k + 1)],
                                           start=(k == 0), stop=last)
                            if last:
                                ins.then_inc(s_te.h, 1)
                            return ins
                        return f
                    emit("te", mk())
                t_xbg = s_te.n
                wait("act", s_te, t_xbg)
                t_xbe = 0
                for q in range(GRP):
                    t_xbe = s_act.tok()
                    emit("act", lambda e, g=g, q=q: e.copy(
                        xb2[:, 512 * (g % 2) + 128 * q:512 * (g % 2) + 128 * (q + 1)],
                        P[4][:, 128 * q:128 * (q + 1)]).then_inc(s_act.h, 1))
                bank_set_free(4, s_act, t_xbe)
                open_mm[g] = mm_count[0]
                scan_limit[0] = 256 * (g + 1)
                pump_scan()

        wslot_free = [None, None, None, None]
        wseq = [0]

        def load_weight(src_ap, ncols, chunk_key):
            i = wseq[0] % 4
            wseq[0] += 1
            if chunk_key not in scw_tok:
                wait_free("pool", wslot_free[i])
                t = dma("pool", s_ws[i], wsl[i][:, 0:ncols], WSRC[chunk_key])
                wait("sp", s_ws[i], t)
                t_w = dma("sp", s_scws[i], WSCR[chunk_key], wsl[i][:, 0:ncols])
                scw_tok[chunk_key] = (s_scws[i], t_w)
                slot_extra[i] = (s_scws[i], t_w)
                return i, (s_ws[i], t)
            q = "pool" if chunk_key[0] == "h2" else "sp"
            wait_free(q, wslot_free[i])
            wsem, wtok = scw_tok[chunk_key]
            wait(q, wsem, wtok)
            t = dma(q, s_ws[i], wsl[i][:, 0:ncols], WSCR[chunk_key])
            slot_extra[i] = None
            return i, (s_ws[i], t)

        def set_slot_free(i, te_tok):
            if slot_extra[i] is not None:
                wslot_free[i] = ("both", slot_extra[i], (s_te, te_tok))
            else:
                wslot_free[i] = (s_te, te_tok)

        def fmm(closure):
            emit("te", closure)
            mm_count[0] += 1
            pump_scan()

        u_free = [None, None]
        o3_freeL = [None]
        outs_free = [None, None]
        xa_free = [None, None]
        attn_xal = {}
        prev_rs = [0]
        rs_tok = {}
        o3_done_tok = {}
        o3_half1_tok = {}
        yhi_latest = [0]
        pl_chain = [0]
        pending = []

        def emit_xa_load(m, i):
            kt = GRP * i + m
            msl = m % 2
            wait_free("pool", xa_free[msl])
            dma("pool", s_xa[msl], xa[msl][0:64, :], xin[0, 64 * kt:64 * kt + 64, :])
            attn_xal[(i, m)] = (s_xa[msl], dma("pool", s_xa[msl], xa[msl][64:128, :],
                                               xin[1, 64 * kt:64 * kt + 64, :]))

        def emit_attn(i, m, pb4=4, pb5=5):
            wait("te", s_sth, 256 * i + 64 * (m + 1))
            bank_wait("te", pb4)
            bank_wait("te", pb5)
            s_te.tok()
            emit("te", lambda e, i=i, m=m, pb4=pb4: e.matmul(
                P[pb4][:, :], statesT[:, 512 * (i % 2) + 128 * m:512 * (i % 2) + 128 * (m + 1)],
                ctb[:, 0:512], start=True, stop=True).then_inc(s_te.h, 1))
            t_at0 = s_te.n
            s_te.tok()
            emit("te", lambda e, i=i, m=m, pb5=pb5: e.matmul(
                P[pb5][:, :], statesT[:, 512 * (i % 2) + 128 * m:512 * (i % 2) + 128 * (m + 1)],
                ctb[:, 512:1024], start=True, stop=True).then_inc(s_te.h, 1))
            t_at1 = s_te.n
            if m == GRP - 1:
                attn_last_tok[i] = t_at1
            t_ae0 = dve_step(lambda e, m=m, pb4=pb4: e.tensor_copy(
                attn_sb[m][:, 0:512], P[pb4][:, :]), waits=[(s_te, t_at0)])
            bank_set_free(pb4, s_dve, t_ae0)
            t_ae1 = dve_step(lambda e, m=m, pb5=pb5: e.tensor_copy(
                attn_sb[m][:, 512:1024], P[pb5][:, :]), waits=[(s_te, t_at1)])
            bank_set_free(pb5, s_dve, t_ae1)

        def emit_p1(i, m):
            kt = GRP * i + m
            msl = m % 2
            t_xnf = dve_step(lambda e, kt=kt, msl=msl: e.tensor_scalar(
                cbA[:, :], xa[msl][:, :], mu_arr[:, kt:kt + 1], r_arr[:, kt:kt + 1],
                op0=ALU.subtract, op1=ALU.mult),
                waits=[attn_xal[(i, m)]])
            wait("dve", s_dve, t_xnf)
            t_t1 = dve_step(lambda e: e.tensor_tensor(
                cbB[:, :], cbA[:, :], d1B[:, :], op=ALU.mult))
            wait("dve", s_dve, t_t1)
            t_yhi = dve_step(lambda e, m=m: e.tensor_tensor(
                cbC[:, :], cbB[:, :], attn_sb[m][:, :], op=ALU.add))
            wait("dve", s_dve, t_yhi)
            for fn in ((lambda e, msl=msl: e.bn_stats(bn6C[msl][:, 0:6], cbC[:, 0:512])),
                       (lambda e, msl=msl: e.bn_stats(bn6C[msl][:, 6:12], cbC[:, 512:1024])),
                       (lambda e, msl=msl: e.bn_aggr(mvC[msl][:, :], bn6C[msl][:, :]))):
                tok = dve_step(fn)
                wait("dve", s_dve, tok)
            rs_tok[(i, m)] = newton_rsqrt(mvC[msl][:, 1:2], r2r[msl][:, :], msl,
                                          wset=(nwC[msl], nwiC[msl]))

        def emit_p2(i, m):
            kt = GRP * i + m
            msl = m % 2
            wait("dve", s_dve, rs_tok[(i, m)])
            t_t2 = dve_step(lambda e, msl=msl: e.tensor_scalar(
                cbA[:, :], cbC[:, :], mvC[msl][:, 0:1], r2r[msl][:, :],
                op0=ALU.subtract, op1=ALU.mult))
            wait("dve", s_dve, t_t2)
            t_t3 = dve_step(lambda e: e.tensor_tensor(
                cbB[:, :], cbA[:, :], gammaB[:, :], op=ALU.mult))
            wait("dve", s_dve, t_t3)
            t_t4 = dve_step(lambda e: e.tensor_tensor(
                cbC[:, :], cbB[:, :], betaB[:, :], op=ALU.add))
            wait("dve", s_dve, t_t4)
            t_o1 = dve_step(lambda e, msl=msl: e.tensor_tensor(
                cbA[:, :], cbC[:, :], xa[msl][:, :], op=ALU.add))
            wait("dve", s_dve, t_o1)
            xa_free[msl] = (s_dve, t_o1)
            w_ = [outs_free[msl]] if outs_free[msl] is not None else []
            t_out = dve_step(lambda e, msl=msl, m=m: e.tensor_tensor(
                outs[msl][:, :], cbA[:, :], o3_sb[:, 1024 * m:1024 * (m + 1)],
                op=ALU.add), waits=w_)
            if m == GRP - 1:
                o3_freeL[0] = (s_dve, t_out)
            wait("pool", s_dve, t_out)
            dma("pool", s_st[msl], out_d[0, 64 * kt:64 * kt + 64, :], outs[msl][0:64, :])
            t_st = dma("pool", s_st[msl], out_d[1, 64 * kt:64 * kt + 64, :], outs[msl][64:128, :])
            outs_free[msl] = (s_st[msl], t_st)

        def pop_one():
            kind, i_c, m_c = pending.pop(0)
            if kind == "attn":
                emit_attn(i_c, m_c)
            elif kind == "p1":
                emit_p1(i_c, m_c)
            else:
                emit_p2(i_c, m_c)
                if m_c + 2 < GRP:
                    emit_xa_load(m_c + 2, i_c)

        def emit_prologue_te(kt):
            for c in range(4):
                emit_prologue_te_chunk(kt, c)

        t_cl = emit_sp_consts()
        wait("dve", s_cl, t_cl)
        tok_d1b = s_dve.tok()
        emit("dve", lambda e: e.tensor_scalar(d1B[:, :], cbA[:, :], 1.0, None, op0=ALU.add).then_inc(s_dve.h, 1))
        wait("sp", s_cl, t_cl)
        for kt in range(GRP):
            emit_prologue_load(kt)
            emit_prologue_te(kt)
        emit_prologue_load(GRP + 0)
        emit_prologue_load(GRP + 1)

        W12ROT = [0, 1, 2, 5]
        POP_SLOTS = (0, 1, 2, 3, 6, 8, 12, 14, 18, 20, 24, 26)

        for i in range(NFT):
            isl = i % 2
            last_tile = (i == NFT - 1)
            pro_loads = {7: 2, 13: 3}
            pro_tes = {3: (0, 0), 4: (0, 1), 5: (0, 2), 6: (0, 3),
                       9: (1, 0), 10: (1, 1), 11: (1, 2), 12: (1, 3),
                       15: (2, 0), 16: (2, 1), 17: (2, 2), 18: (2, 3),
                       21: (3, 0), 22: (3, 1), 23: (3, 2), 24: (3, 3)}
            for j in range(32):
                if j in POP_SLOTS and pending:
                    pop_one()
                if j % 2 == 0:
                    sw12, t_w12 = load_weight(None, 4096, ("w12", j // 2))
                off = 2048 * (j % 2)
                w1ap = wsl[sw12][:, off:off + 1024]
                w2ap = wsl[sw12][:, off + 1024:off + 2048]
                b1 = W12ROT[(2 * j) % 4]
                b2 = W12ROT[(2 * j + 1) % 4]
                if j == 0:
                    wait("te", s_dve, group_xnT_tok[i])
                if j % 2 == 0:
                    wait("te", t_w12[0], t_w12[1])
                bank_wait("te", b1)
                for k in range(8):
                    last = (k == 7)
                    if last:
                        s_te.tok()
                    def mk(k=k, b1=b1, w1ap=w1ap, isl=isl, last=last):
                        def f(e):
                            ins = e.matmul(P[b1][:, :], w1ap[:, 128 * k:128 * (k + 1)],
                                           xnT[isl][:, 512 * k:512 * (k + 1)],
                                           start=(k == 0), stop=last)
                            if last:
                                ins.then_inc(s_te.h, 1)
                            return ins
                        return f
                    fmm(mk())
                t_g1 = s_te.n
                bank_wait("te", b2)
                for k in range(8):
                    last = (k == 7)
                    if last:
                        s_te.tok()
                    def mk(k=k, b2=b2, w2ap=w2ap, isl=isl, last=last):
                        def f(e):
                            ins = e.matmul(P[b2][:, :], w2ap[:, 128 * k:128 * (k + 1)],
                                           xnT[isl][:, 512 * k:512 * (k + 1)],
                                           start=(k == 0), stop=last)
                            if last:
                                ins.then_inc(s_te.h, 1)
                            return ins
                        return f
                    fmm(mk())
                t_g2 = s_te.n
                if j % 2 == 1:
                    set_slot_free(sw12, t_g2)
                wait("act", s_te, t_g1)
                wait_free("act", u_free[j % 2])
                t_u = s_act.tok()
                emit("act", lambda e, j=j, b1=b1: e.activation(
                    u_sb[j % 2][:, :], P[b1][:, :], AF.Silu,
                    scale=s1c[:, j:j + 1]).then_inc(s_act.h, 1))
                bank_set_free(b1, s_act, t_u)
                wait("act", s_te, t_g2)
                t_v0 = s_act.tok()
                emit("act", lambda e, j=j, b2=b2: e.activation(
                    v_sb[j % 2][:, :], P[b2][:, :], AF.Identity,
                    scale=s2c[:, j:j + 1]).then_inc(s_act.h, 1))
                bank_set_free(b2, s_act, t_v0)
                t_v = s_act.tok()
                emit("act", lambda e, j=j: e.copy(
                    tch[:, 0:1], v_sb[j % 2][:, 0:1]).then_inc(s_act.h, 1))
                wait("dve", s_act, t_v)
                t_a = s_dve.tok()
                emit("dve", lambda e, j=j: e.tensor_tensor(
                    a_sb[:, 512 * j:512 * (j + 1)], u_sb[j % 2][:, :],
                    v_sb[j % 2][:, :], op=ALU.mult).then_inc(s_dve.h, 1))
                u_free[j % 2] = (s_dve, t_a)
                if j == 31:
                    wait("dve", s_dve, t_a)
                    a_done_tok = s_dve.tok()
                    emit("dve", lambda e: e.tensor_copy(
                        tch[:, 0:1], a_sb[:, 16382:16384].bitcast(F32)[:, 0:1]
                    ).then_inc(s_dve.h, 1))
                if i + 1 < NFT:
                    if j in pro_loads:
                        emit_prologue_load(GRP * (i + 1) + pro_loads[j])
                    if j in pro_tes:
                        kt_l, c_l = pro_tes[j]
                        emit_prologue_te_chunk(GRP * (i + 1) + kt_l, c_l)
            xnT_free[isl] = (s_te, t_g2)

            if DEBUG and i == 0:
                wait("sp", s_dve, a_done_tok)
                t_dbg = dma("sp", s_st[0], dbg_a[:, :], a_sb[:, :])
                wait("te", s_st[0], t_dbg)

            for j in range(32):
                swh, t_wh = load_weight(None, 4096, ("h1", j))
                bk = j % 2
                wait("te", t_wh[0], t_wh[1])
                bank_wait("te", bk)
                if j == 0:
                    wait("te", s_dve, a_done_tok)
                for k in range(32):
                    last = (k == 31)
                    if last:
                        s_te.tok()
                    def mk(k=k, bk=bk, swh=swh, last=last):
                        def f(e):
                            ins = e.matmul(P[bk][:, :], wsl[swh][:, 128 * k:128 * (k + 1)],
                                           a_sb[:, 512 * k:512 * (k + 1)],
                                           start=(k == 0), stop=last)
                            if last:
                                ins.then_inc(s_te.h, 1)
                            return ins
                        return f
                    fmm(mk())
                t_g = s_te.n
                set_slot_free(swh, t_g)
                wait("act", s_te, t_g)
                t_gelu = s_act.tok()
                emit("act", lambda e, j=j, bk=bk: e.activation(
                    g1_sb[:, 512 * j:512 * (j + 1)], P[bk][:, :], AF.Gelu,
                    scale=h1sc[:, j:j + 1]).then_inc(s_act.h, 1))
                bank_set_free(bk, s_act, t_gelu)
                if j == 8 and i + 2 < NFT:
                    emit_prologue_load(GRP * (i + 2) + 0)
            h1_gelu_done = s_act.tok()
            emit("act", lambda e: e.copy(
                tch[:, 0:1], g1_sb[:, 16382:16384].bitcast(F32)[:, 0:1]).then_inc(s_act.h, 1))

            if DEBUG and i == 0:
                wait("sp", s_act, h1_gelu_done)
                t_dbg = dma("sp", s_st[0], dbg_g1[:, :], g1_sb[:, :])
                wait("te", s_st[0], t_dbg)

            for j in range(32):
                swh, t_wh = load_weight(None, 4096, ("h2", j))
                bk = j % 2
                wait("te", t_wh[0], t_wh[1])
                bank_wait("te", bk)
                if j == 0:
                    wait("te", s_act, h1_gelu_done)
                for k in range(32):
                    last = (k == 31)
                    if last:
                        s_te.tok()
                    def mk(k=k, bk=bk, swh=swh, last=last):
                        def f(e):
                            ins = e.matmul(P[bk][:, :], wsl[swh][:, 128 * k:128 * (k + 1)],
                                           g1_sb[:, 512 * k:512 * (k + 1)],
                                           start=(k == 0), stop=last)
                            if last:
                                ins.then_inc(s_te.h, 1)
                            return ins
                        return f
                    fmm(mk())
                t_g = s_te.n
                set_slot_free(swh, t_g)
                wait("act", s_te, t_g)
                t_gelu = s_act.tok()
                emit("act", lambda e, j=j, bk=bk: e.activation(
                    a_sb[:, 512 * j:512 * (j + 1)], P[bk][:, :], AF.Gelu,
                    scale=h2sc[:, j:j + 1]).then_inc(s_act.h, 1))
                bank_set_free(bk, s_act, t_gelu)
                if j == 8 and i + 2 < NFT:
                    emit_prologue_load(GRP * (i + 2) + 1)
            g2_done_tok = s_act.tok()
            emit("act", lambda e: e.copy(
                tch[:, 0:1], a_sb[:, 16382:16384].bitcast(F32)[:, 0:1]).then_inc(s_act.h, 1))

            if DEBUG and i == 0:
                wait("sp", s_act, g2_done_tok)
                t_dbg = dma("sp", s_st[0], dbg_g2[:, :], a_sb[:, :])
                wait("te", s_st[0], t_dbg)

            if last_tile:
                emit_xa_load(0, i)
                emit_xa_load(1, i)

            W3B = [0, 1, 2, 5]
            for half in range(2):
                for m in range(4):
                    bank_wait("te", W3B[m])
                for kg in range(4):
                    sw3, t_w3 = load_weight(None, 4096, ("w3", half, kg))
                    wait("te", t_w3[0], t_w3[1])
                    if half == 0 and kg == 0:
                        wait("te", s_act, g2_done_tok)
                    for kl in range(8):
                        k = 8 * kg + kl
                        for m in range(4):
                            last = (kl == 7 and m == 3)
                            stop = (kg == 3 and kl == 7)
                            if last:
                                s_te.tok()
                            def mk(k=k, kl=kl, m=m, sw3=sw3, last=last, stop=stop,
                                   first=(k == 0)):
                                def f(e):
                                    ins = e.matmul(
                                        P[W3B[m]][:, :],
                                        a_sb[:, 512 * k + 128 * m:512 * k + 128 * (m + 1)],
                                        wsl[sw3][:, 512 * kl:512 * (kl + 1)],
                                        start=first, stop=stop)
                                    if last:
                                        ins.then_inc(s_te.h, 1)
                                    return ins
                                return f
                            fmm(mk())
                    set_slot_free(sw3, s_te.n)
                t_w3h = s_te.n
                for m in range(4):
                    wait("dve", s_te, t_w3h)
                    if half == 0 and m == 0 and o3_freeL[0] is not None:
                        wait_free("dve", o3_freeL[0])
                    t_e = s_dve.tok()
                    emit("dve", lambda e, m=m, half=half: e.tensor_tensor(
                        o3_sb[:, 1024 * m + 512 * half:1024 * m + 512 * (half + 1)],
                        P[W3B[m]][:, :], s3B[:, 512 * half:512 * (half + 1)],
                        op=ALU.mult).then_inc(s_dve.h, 1))
                    bank_set_free(W3B[m], s_dve, t_e)
                    if half == 1:
                        o3_half1_tok[m] = dve_step(lambda e, m=m: e.tensor_copy(
                            tch[:, 0:2], o3_sb[:, 1024 * m + 511:1024 * m + 1024:512]))
                if half == 1:
                    o3_done_tok[i] = o3_half1_tok[3]

            if last_tile:
                drain_scan(SEQ)
                for m in range(GRP):
                    emit_attn(i, m)
                for m in range(GRP):
                    emit_p1(i, m)
                    emit_p2(i, m)
                    if m + 2 < GRP:
                        emit_xa_load(m + 2, i)
            else:
                emit_xa_load(0, i)
                emit_xa_load(1, i)
                for m in range(GRP):
                    pending.append(("attn", i, m))
                for m in range(GRP):
                    pending.append(("p1", i, m))
                    pending.append(("p2", i, m))

        assert not pending, f"{len(pending)} entries left in combine queue"

        if DEBUG:
            wait("sp", s_sth, SEQ)
            dma("sp", s_st[0], dbg_mu[:, :], mu_arr[:, :])
            dma("sp", s_st[0], dbg_r[:, :], r_arr[:, :])

        emit("pool", lambda e: e.wait_ge(s_st[0].h, s_st[0].n))
        emit("pool", lambda e: e.wait_ge(s_st[1].h, s_st[1].n))

        block = stack.enter_context(nc.Block())

        @block.sync
        def _(sync):
            for f in E["sp"]:
                f(sync)

        @block.gpsimd
        def _(gpsimd):
            for f in E["pool"]:
                f(gpsimd)

        @block.tensor
        def _(tensor):
            for f in E["te"]:
                f(tensor)

        @block.scalar
        def _(scalar):
            for f in E["act"]:
                f(scalar)

        @block.vector
        def _(vector):
            for f in E["dve"]:
                f(vector)

    return nc


def _prep_shared(L0_attn_A, L0_attn_B, L0_attn_C, L0_attn_D, L0_attn_gamma,
                 L0_attn_beta, L0_ffn_w1_w, L0_ffn_w1_s, L0_ffn_w2_w, L0_ffn_w2_s,
                 L0_ffn_h1_w, L0_ffn_h1_s, L0_ffn_h2_w, L0_ffn_h2_s,
                 L0_ffn_w3_w, L0_ffn_w3_s):
    c = np.ascontiguousarray
    f32 = np.float32
    def tile_w12(w):
        return c(w.reshape(32, 128, 8, 128).transpose(0, 3, 2, 1).reshape(32, 128, 8 * 128)).astype(f32)
    def tile_h(w):
        return c(w.reshape(32, 128, 32, 128).transpose(0, 3, 2, 1).reshape(32, 128, 32 * 128)).astype(f32)
    def tile_w3(w):
        return c(w.reshape(2, 512, 4, 8, 128).transpose(0, 2, 4, 3, 1).reshape(2, 4, 128, 8 * 512)).astype(f32)
    w1t = tile_w12(np.asarray(L0_ffn_w1_w))
    w2t = tile_w12(np.asarray(L0_ffn_w2_w))
    w12 = np.empty((16, 128, 4096), np.float32)
    w12[:, :, 0:1024] = w1t[0::2]
    w12[:, :, 1024:2048] = w2t[0::2]
    w12[:, :, 2048:3072] = w1t[1::2]
    w12[:, :, 3072:4096] = w2t[1::2]
    return {
        "w12p": np.ascontiguousarray(w12),
        "h1p": tile_h(np.asarray(L0_ffn_h1_w)),
        "h2p": tile_h(np.asarray(L0_ffn_h2_w)),
        "w3p": tile_w3(np.asarray(L0_ffn_w3_w)),
        "btp": c(np.asarray(L0_attn_B).T.reshape(8, 128, 128).transpose(1, 0, 2).reshape(128, 8 * 128)).astype(f32),
        "ctp": c(np.asarray(L0_attn_C).T).astype(f32),
        "atp": c(np.asarray(L0_attn_A).T).astype(f32),
        "identp": np.eye(128, dtype=bfloat16),
        "gammap": c(np.broadcast_to(np.asarray(L0_attn_gamma), (128, D))).astype(f32),
        "betap": c(np.broadcast_to(np.asarray(L0_attn_beta), (128, D))).astype(f32),
        "dp": c(np.broadcast_to(np.asarray(L0_attn_D), (128, D))).astype(f32),
        "s3p": c(np.broadcast_to(np.asarray(L0_ffn_w3_s), (128, D))).astype(f32),
        "s1p": c(np.asarray(L0_ffn_w1_s).reshape(32, 128).T).astype(f32),
        "s2p": c(np.asarray(L0_ffn_w2_s).reshape(32, 128).T).astype(f32),
        "h1sp": c(np.asarray(L0_ffn_h1_s).reshape(32, 128).T).astype(f32),
        "h2sp": c(np.asarray(L0_ffn_h2_s).reshape(32, 128).T).astype(f32),
    }


def kernel(L0_attn_x, L0_attn_A, L0_attn_B, L0_attn_C, L0_attn_D, L0_attn_gamma,
           L0_attn_beta, L0_ffn_w1_w, L0_ffn_w1_s, L0_ffn_w2_w, L0_ffn_w2_s,
           L0_ffn_h1_w, L0_ffn_h1_s, L0_ffn_h2_w, L0_ffn_h2_s,
           L0_ffn_w3_w, L0_ffn_w3_s):
    x = np.ascontiguousarray(np.asarray(L0_attn_x, dtype=np.float32))
    shared = _prep_shared(L0_attn_A, L0_attn_B, L0_attn_C, L0_attn_D, L0_attn_gamma,
                          L0_attn_beta, L0_ffn_w1_w, L0_ffn_w1_s, L0_ffn_w2_w,
                          L0_ffn_w2_s, L0_ffn_h1_w, L0_ffn_h1_s, L0_ffn_h2_w,
                          L0_ffn_h2_s, L0_ffn_w3_w, L0_ffn_w3_s)
    nc = build_program()
    in_maps = []
    for c in range(8):
        m = dict(shared)
        m["xin"] = np.ascontiguousarray(x[2 * c:2 * c + 2])
        in_maps.append(m)
    res = run_bass_kernel_spmd(nc, in_maps, core_ids=list(range(8)))
    out = np.empty((16, SEQ, D), np.float32)
    for c in range(8):
        out[2 * c:2 * c + 2] = res.results[c]["out"]
    return out


# version 33
# speedup vs baseline: 1.0139x; 1.0139x over previous
import numpy as np
import ml_dtypes
import concourse.bass as bass
import concourse.mybir as mybir
from concourse.bass_utils import run_bass_kernel_spmd
from contextlib import ExitStack

F32 = mybir.dt.float32
BF16 = mybir.dt.bfloat16
I32 = mybir.dt.int32
AF = mybir.ActivationFunctionType
ALU = mybir.AluOpType
bfloat16 = ml_dtypes.bfloat16

D = 1024
H = 4096
SEQ = 2048
BPC = 2
NTOK = BPC * SEQ
NTILE = 32
TFF = 512
NFT = NTOK // TFF
GRP = TFF // 128
EPS = 1e-5
MAGIC = float(0x5F3759DF)
SCAN_EVERY = 9
DEBUG = False


class S:
    def __init__(self, stack, nc, name):
        self.h = stack.enter_context(nc.semaphore(name))
        self.n = 0
    def tok(self, k=1):
        self.n += k
        return self.n


def build_program():
    nc = bass.Bass("TRN2", target_bir_lowering=False, debug=False)

    xin = nc.dram_tensor("xin", [BPC, SEQ, D], F32, kind="ExternalInput").ap()
    w12p = nc.dram_tensor("w12p", [16, 128, 4096], F32, kind="ExternalInput").ap()
    h1p = nc.dram_tensor("h1p", [32, 128, 32 * 128], F32, kind="ExternalInput").ap()
    h2p = nc.dram_tensor("h2p", [32, 128, 32 * 128], F32, kind="ExternalInput").ap()
    w3p = nc.dram_tensor("w3p", [2, 4, 128, 8 * 512], F32, kind="ExternalInput").ap()
    btp = nc.dram_tensor("btp", [128, 8 * 128], F32, kind="ExternalInput").ap()
    ctp = nc.dram_tensor("ctp", [128, D], F32, kind="ExternalInput").ap()
    atp = nc.dram_tensor("atp", [128, 128], F32, kind="ExternalInput").ap()
    identp = nc.dram_tensor("identp", [128, 128], BF16, kind="ExternalInput").ap()
    gammap = nc.dram_tensor("gammap", [128, D], F32, kind="ExternalInput").ap()
    betap = nc.dram_tensor("betap", [128, D], F32, kind="ExternalInput").ap()
    dp = nc.dram_tensor("dp", [128, D], F32, kind="ExternalInput").ap()
    s3p = nc.dram_tensor("s3p", [128, D], F32, kind="ExternalInput").ap()
    s1p = nc.dram_tensor("s1p", [128, 32], F32, kind="ExternalInput").ap()
    s2p = nc.dram_tensor("s2p", [128, 32], F32, kind="ExternalInput").ap()
    h1sp = nc.dram_tensor("h1sp", [128, 32], F32, kind="ExternalInput").ap()
    h2sp = nc.dram_tensor("h2sp", [128, 32], F32, kind="ExternalInput").ap()
    out_d = nc.dram_tensor("out", [BPC, SEQ, D], F32, kind="ExternalOutput").ap()
    SCR = "ExternalOutput" if DEBUG else "Internal"
    wb12s = nc.dram_tensor("wb12s", [16, 128, 4096], BF16, kind=SCR).ap()
    wh1s = nc.dram_tensor("wh1s", [32, 128, 32 * 128], BF16, kind=SCR).ap()
    wh2s = nc.dram_tensor("wh2s", [32, 128, 32 * 128], BF16, kind=SCR).ap()
    w3s = nc.dram_tensor("w3s", [2, 4, 128, 8 * 512], BF16, kind=SCR).ap()
    if DEBUG:
        dbg_st = nc.dram_tensor("dbg_st", [128, NTOK], BF16, kind="ExternalOutput").ap()
        dbg_mu = nc.dram_tensor("dbg_mu", [128, 32], F32, kind="ExternalOutput").ap()
        dbg_r = nc.dram_tensor("dbg_r", [128, 32], F32, kind="ExternalOutput").ap()
        dbg_a = nc.dram_tensor("dbg_a", [128, 32 * 512], BF16, kind="ExternalOutput").ap()
        dbg_g1 = nc.dram_tensor("dbg_g1", [128, 32 * 512], BF16, kind="ExternalOutput").ap()
        dbg_g2 = nc.dram_tensor("dbg_g2", [128, 32 * 512], BF16, kind="ExternalOutput").ap()

    with ExitStack() as stack:
        sb = lambda n_, s, d: stack.enter_context(nc.sbuf_tensor(n_, s, d))
        psa = lambda n_, s, d: stack.enter_context(nc.psum_tensor(n_, s, d))

        gammaB = sb("gammaB", [128, D], F32)
        betaB = sb("betaB", [128, D], F32)
        d1B = sb("d1B", [128, D], F32)
        s3B = sb("s3B", [128, D], F32)
        ctb = sb("ctb", [128, D], BF16)
        btb = sb("btb", [128, 8 * 128], BF16)
        atb = sb("atb", [128, 128], BF16)
        ident = sb("ident", [128, 128], BF16)
        s1c = sb("s1c", [128, 32], F32)
        s2c = sb("s2c", [128, 32], F32)
        h1sc = sb("h1sc", [128, 32], F32)
        h2sc = sb("h2sc", [128, 32], F32)
        xt = [sb(f"xt{i}", [128, D], F32) for i in range(2)]
        xa = [sb(f"xa{i}", [128, D], F32) for i in range(2)]
        xn_a = [sb(f"xn_a{i}", [128, D], BF16) for i in range(2)]
        xnT = [sb(f"xnT{i}", [128, 8 * 512], BF16) for i in range(2)]
        wsl = [sb(f"wsl{i}", [128, 4096], BF16) for i in range(4)]
        a_sb = sb("a_sb", [128, 32 * 512], BF16)
        g1_sb = sb("g1_sb", [128, 32 * 512], BF16)
        attn_sb = [g1_sb[:, 8192 + 2048 * m:8192 + 2048 * (m + 1)].bitcast(F32)
                   for m in range(4)]
        u_sb = [sb(f"u_sb{i}", [128, 512], F32) for i in range(2)]
        v_sb = [sb(f"v_sb{i}", [128, 512], F32) for i in range(2)]
        o3_sb = sb("o3_sb", [128, 4 * D], F32)
        cbA = sb("cbA", [128, D], F32)
        cbB = sb("cbB", [128, D], F32)
        cbC = sb("cbC", [128, D], F32)
        outs = [sb(f"outs{i}", [128, D], F32) for i in range(2)]
        xb2 = sb("xb2", [128, 2 * 512], F32)
        statesT = sb("statesT", [128, 2 * 512], BF16)
        mu_arr = sb("mu_arr", [128, 32], F32)
        r_arr = sb("r_arr", [128, 32], F32)
        mr_arr = sb("mr_arr", [128, 32], F32)
        bn6 = [sb(f"bn6_{i}", [128, 12], F32) for i in range(2)]
        mv = [sb(f"mv{i}", [128, 2], F32) for i in range(2)]
        nw = [sb(f"nw{i}", [128, 6], F32) for i in range(2)]
        nwi = [sb(f"nwi{i}", [128, 1], I32) for i in range(2)]
        bn6C = [sb(f"bn6C{i}", [128, 12], F32) for i in range(2)]
        mvC = [sb(f"mvC{i}", [128, 2], F32) for i in range(2)]
        nwC = [sb(f"nwC{i}", [128, 6], F32) for i in range(2)]
        nwiC = [sb(f"nwiC{i}", [128, 1], I32) for i in range(2)]
        r2r = [sb(f"r2r{i}", [128, 1], F32) for i in range(2)]
        mr2r = [sb(f"mr2r{i}", [128, 1], F32) for i in range(2)]
        tch = sb("tch", [128, 2], F32)

        P = [psa(f"P{i}", [128, 512], F32) for i in range(6)]
        TB = [psa(f"TB{i}", [128, 1024], BF16) for i in range(2)]

        s_cast = S(stack, nc, "s_cast")
        s_pc = S(stack, nc, "s_pc")
        s_cl = S(stack, nc, "s_cl")
        s_xt = [S(stack, nc, f"s_xt{i}") for i in range(2)]
        s_xa = [S(stack, nc, f"s_xa{i}") for i in range(2)]
        s_ws = [S(stack, nc, f"s_ws{i}") for i in range(4)]
        s_scws = [S(stack, nc, f"s_scw{i}") for i in range(4)]
        s_st = [S(stack, nc, f"s_st{i}") for i in range(2)]
        s_te = S(stack, nc, "s_te")
        s_act = S(stack, nc, "s_act")
        s_dve = S(stack, nc, "s_dve")
        s_plc = S(stack, nc, "s_plc")
        s_smm = S(stack, nc, "s_smm")
        s_sth = S(stack, nc, "s_sth")

        E = {k: [] for k in ("sp", "pool", "te", "act", "dve")}

        def emit(eng, fn):
            E[eng].append(fn)

        def wait(eng, sem, val):
            if val > 0:
                emit(eng, lambda e, s=sem.h, v=val: e.wait_ge(s, v))

        bank_free = {}

        def bank_wait(eng, bk):
            if bk in bank_free:
                sem, val = bank_free[bk]
                wait(eng, sem, val)

        def bank_set_free(bk, sem, val):
            bank_free[bk] = (sem, val)

        def wait_free(eng, entry):
            if entry is None:
                return
            if entry[0] == "both":
                _, a, b_ = entry
                wait(eng, a[0], a[1])
                wait(eng, b_[0], b_[1])
            else:
                wait(eng, entry[0], entry[1])

        def dma(eng, sem, out, in_):
            t = sem.tok(16)
            emit(eng, lambda e, o=out, i=in_, s=sem.h: e.dma_start(out=o, in_=i).then_inc(s, 16))
            return t

        def dve_step(fn, waits=()):
            for sem, val in waits:
                wait("dve", sem, val)
            tok = s_dve.tok()
            emit("dve", lambda e, f=fn: f(e).then_inc(s_dve.h, 1))
            return tok

        def pool_step(fn, waits=()):
            for sem, val in waits:
                wait("pool", sem, val)
            tok = s_plc.tok()
            emit("pool", lambda e, f=fn: f(e).then_inc(s_plc.h, 1))
            return tok

        t_pc = 0
        for dst, src in ((ctb, ctp), (btb, btp), (atb, atp), (ident, identp)):
            t_pc = dma("pool", s_pc, dst[:, :], src[:, :])

        def emit_sp_consts():
            t = 0
            for dst, src in ((gammaB, gammap), (betaB, betap), (s3B, s3p),
                             (s1c, s1p), (s2c, s2p),
                             (h1sc, h1sp), (h2sc, h2sp), (cbA, dp)):
                t = dma("sp", s_cl, dst[:, :], src[:, :])
            return t

        WSRC = {}
        WSCR = {}
        for jj in range(16):
            WSRC[("w12", jj)] = w12p[jj]
            WSCR[("w12", jj)] = wb12s[jj]
        for j in range(32):
            WSRC[("h1", j)] = h1p[j]
            WSCR[("h1", j)] = wh1s[j]
            WSRC[("h2", j)] = h2p[j]
            WSCR[("h2", j)] = wh2s[j]
        for half in range(2):
            for kg in range(4):
                WSRC[("w3", half, kg)] = w3p[half, kg]
                WSCR[("w3", half, kg)] = w3s[half, kg]
        scw_tok = {}
        slot_extra = [None, None, None, None]

        def newton_rsqrt(var_ap, out_ap, ring, wset=None):
            w, wi = wset if wset is not None else (nw[ring], nwi[ring])
            ve = w[:, 0:1]
            ta = w[:, 1:2]
            ys = [w[:, 2:3], w[:, 3:4]]
            yy = w[:, 4:5]
            seq = []
            seq.append(lambda e: e.tensor_scalar(ve, var_ap, EPS, None, op0=ALU.add))
            seq.append(lambda e: e.tensor_scalar(wi[:, :], ve.bitcast(I32), 1, None,
                                                 op0=ALU.logical_shift_right))
            seq.append(lambda e: e.tensor_scalar(ta, wi[:, :], -1.0, MAGIC,
                                                 op0=ALU.mult, op1=ALU.add))
            seq.append(lambda e: e.tensor_copy(wi[:, :], ta))
            seq.append(lambda e: e.tensor_copy(ys[0], wi[:, :].bitcast(F32)))
            for it in range(2):
                cur = ys[it % 2]
                dst = out_ap if it == 1 else ys[(it + 1) % 2]
                seq.append(lambda e, cur=cur: e.tensor_tensor(yy, cur, cur, op=ALU.mult))
                seq.append(lambda e: e.scalar_tensor_tensor(ta, yy, -0.5, ve,
                                                            op0=ALU.mult, op1=ALU.mult))
                seq.append(lambda e, d=dst, cur=cur: e.scalar_tensor_tensor(
                    d, ta, 1.5, cur, op0=ALU.add, op1=ALU.mult))
            last_tok = None
            for k, f in enumerate(seq):
                tok = s_dve.tok()
                emit("dve", lambda e, f=f: f(e).then_inc(s_dve.h, 1))
                if k < len(seq) - 1:
                    wait("dve", s_dve, tok)
                last_tok = tok
            return last_tok

        scan_next = [0]
        scan_limit = [0]
        mm_count = [0]
        xb_evac_tok = {}
        open_mm = {0: 0}
        scan_due_prev = [-SCAN_EVERY]

        def scan_due(t):
            w = t // 256
            return max(scan_due_prev[0] + SCAN_EVERY, open_mm[w] + 2)

        attn_last_tok = {}

        def st_col(t):
            w = t // 256
            kt_g = t // 64
            return 512 * (w % 2) + 128 * (kt_g % 4) + (t % 64)

        def emit_scan_step(t):
            kt_g = t // 64
            slot = (kt_g // 4) % 2
            c0s = 512 * slot + 128 * (kt_g % 4) + (t % 64)
            c0w = st_col(t)
            w = t // 256
            if t % 256 == 0 and w >= 2:
                wait("act", s_te, attn_last_tok[w - 2])
            if t % 256 == 0:
                wait("act", s_act, xb_evac_tok[w])
            if t == 0:
                s_sth.tok()
                emit("act", lambda e, c0s=c0s: e.activation(
                    statesT[:, 0:1], xb2[:, c0s:c0s + 1], AF.Tanh,
                    scale=0.0, bias=xb2[:, c0s:c0s + 1]).then_inc(s_sth.h, 1))
                s_sth.tok()
                emit("act", lambda e, c0s=c0s: e.activation(
                    statesT[:, 64:65], xb2[:, c0s + 64:c0s + 65], AF.Tanh,
                    scale=0.0, bias=xb2[:, c0s + 64:c0s + 65]).then_inc(s_sth.h, 1))
                return
            p0w = st_col(t - 1)
            sl = t % 2
            wait("te", s_sth, 2 * t)
            t_mm = s_smm.tok()
            emit("te", lambda e, sl=sl, p0w=p0w: e.matmul(
                P[3][:, 2 * sl:2 * sl + 2], atb[:, :],
                statesT[:, p0w:p0w + 65:64], start=True, stop=True).then_inc(s_smm.h, 1))
            wait("act", s_smm, t_mm)
            s_sth.tok()
            emit("act", lambda e, sl=sl, c0w=c0w, c0s=c0s: e.activation(
                statesT[:, c0w:c0w + 1], P[3][:, 2 * sl:2 * sl + 1], AF.Tanh,
                bias=xb2[:, c0s:c0s + 1]).then_inc(s_sth.h, 1))
            s_sth.tok()
            emit("act", lambda e, sl=sl, c0w=c0w, c0s=c0s: e.activation(
                statesT[:, c0w + 64:c0w + 65], P[3][:, 2 * sl + 1:2 * sl + 2], AF.Tanh,
                bias=xb2[:, c0s + 64:c0s + 65]).then_inc(s_sth.h, 1))

        def pump_scan():
            while scan_next[0] < scan_limit[0]:
                d = scan_due(scan_next[0])
                if d > mm_count[0]:
                    break
                emit_scan_step(scan_next[0])
                scan_due_prev[0] = d
                scan_next[0] += 1

        def drain_scan(upto):
            while scan_next[0] < upto:
                emit_scan_step(scan_next[0])
                scan_next[0] += 1

        xt_free = [None, None]
        xna_free = [None, None]
        xnT_free = [None, None]
        group_xnT_tok = {}
        tp_evac_tok = {}
        pro_xna_tok = {}

        def emit_prologue_load(kt):
            sl = kt % 2
            wait_free("sp", xt_free[sl])
            dma("sp", s_xt[sl], xt[sl][0:64, :], xin[0, 64 * kt:64 * kt + 64, :])
            t_x = dma("sp", s_xt[sl], xt[sl][64:128, :], xin[1, 64 * kt:64 * kt + 64, :])
            wait("dve", s_xt[sl], t_x)
            for fn in [
                (lambda e, sl=sl: e.bn_stats(bn6[sl][:, 0:6], xt[sl][:, 0:512])),
                (lambda e, sl=sl: e.bn_stats(bn6[sl][:, 6:12], xt[sl][:, 512:1024])),
                (lambda e, sl=sl: e.bn_aggr(mv[sl][:, :], bn6[sl][:, :])),
            ]:
                tok = s_dve.tok()
                emit("dve", lambda e, f=fn: f(e).then_inc(s_dve.h, 1))
                wait("dve", s_dve, tok)
            newton_rsqrt(mv[sl][:, 1:2], r_arr[:, kt:kt + 1], sl)
            tok = s_dve.tok()
            emit("dve", lambda e, kt=kt, sl=sl: e.tensor_copy(
                mu_arr[:, kt:kt + 1], mv[sl][:, 0:1]).then_inc(s_dve.h, 1))
            wait("dve", s_dve, tok)
            wait_free("dve", xna_free[sl])
            tok = s_dve.tok()
            emit("dve", lambda e, kt=kt, sl=sl: e.tensor_scalar(
                xn_a[sl][:, :], xt[sl][:, :], mu_arr[:, kt:kt + 1], r_arr[:, kt:kt + 1],
                op0=ALU.subtract, op1=ALU.mult).then_inc(s_dve.h, 1))
            t_xna = tok
            xt_free[sl] = (s_dve, t_xna)
            pro_xna_tok[kt] = t_xna

        def emit_prologue_te_chunk(kt, c):
            sl = kt % 2
            g = kt // GRP
            tloc = kt % GRP
            t_xna = pro_xna_tok[kt]
            if c == 0:
                wait("te", s_pc, t_pc)
                wait("te", s_dve, t_xna)
            t_tp = 0
            for k in range(2 * c, 2 * c + 2):
                bk = ("TB", k % 2)
                bank_wait("te", bk)
                t_tp = s_te.tok()
                emit("te", lambda e, sl=sl, k=k: e.transpose(
                    TB[k % 2][:, 0:128], xn_a[sl][:, 128 * k:128 * (k + 1)], ident[:, :]
                ).then_inc(s_te.h, 1))
                wait("act", s_te, t_tp)
                if tloc == 0 and k == 0:
                    wait_free("act", xnT_free[g % 2])
                t_ev = s_act.tok()
                emit("act", lambda e, g=g, tloc=tloc, k=k: e.copy(
                    xnT[g % 2][:, 512 * k + 128 * tloc:512 * k + 128 * tloc + 128],
                    TB[k % 2][:, 0:128]).then_inc(s_act.h, 1))
                bank_set_free(bk, s_act, t_ev)
            if c == 3:
                xna_free[sl] = (s_te, t_tp)
            if tloc == GRP - 1 and c == 3:
                tokg = t_ev
                group_xnT_tok[g] = tokg
                bank_wait("te", 4)
                wait("te", s_act, tokg)
                for k in range(8):
                    last = (k == 7)
                    if last:
                        s_te.tok()
                    def mk(g=g, k=k, last=last):
                        def f(e):
                            ins = e.matmul(P[4][:, 0:512], btb[:, 128 * k:128 * (k + 1)],
                                           xnT[g % 2][:, 512 * k:512 * (k + 1)],
                                           start=(k == 0), stop=last)
                            if last:
                                ins.then_inc(s_te.h, 1)
                            return ins
                        return f
                    emit("te", mk())
                t_xbg = s_te.n
                wait("act", s_te, t_xbg)
                t_xbe = 0
                for q in range(GRP):
                    t_xbe = s_act.tok()
                    emit("act", lambda e, g=g, q=q: e.copy(
                        xb2[:, 512 * (g % 2) + 128 * q:512 * (g % 2) + 128 * (q + 1)],
                        P[4][:, 128 * q:128 * (q + 1)]).then_inc(s_act.h, 1))
                bank_set_free(4, s_act, t_xbe)
                xb_evac_tok[g] = t_xbe
                open_mm[g] = mm_count[0]
                scan_limit[0] = 256 * (g + 1)
                pump_scan()

        wslot_free = [None, None, None, None]
        wseq = [0]

        def load_weight(src_ap, ncols, chunk_key):
            i = wseq[0] % 4
            wseq[0] += 1
            if chunk_key not in scw_tok:
                wait_free("pool", wslot_free[i])
                t = dma("pool", s_ws[i], wsl[i][:, 0:ncols], WSRC[chunk_key])
                wait("sp", s_ws[i], t)
                t_w = dma("sp", s_scws[i], WSCR[chunk_key], wsl[i][:, 0:ncols])
                scw_tok[chunk_key] = (s_scws[i], t_w)
                slot_extra[i] = (s_scws[i], t_w)
                return i, (s_ws[i], t)
            q = "pool" if chunk_key[0] == "h2" else "sp"
            wait_free(q, wslot_free[i])
            wsem, wtok = scw_tok[chunk_key]
            wait(q, wsem, wtok)
            t = dma(q, s_ws[i], wsl[i][:, 0:ncols], WSCR[chunk_key])
            slot_extra[i] = None
            return i, (s_ws[i], t)

        def set_slot_free(i, te_tok):
            if slot_extra[i] is not None:
                wslot_free[i] = ("both", slot_extra[i], (s_te, te_tok))
            else:
                wslot_free[i] = (s_te, te_tok)

        def fmm(closure):
            emit("te", closure)
            mm_count[0] += 1
            pump_scan()

        u_free = [None, None]
        o3_freeL = [None]
        outs_free = [None, None]
        xa_free = [None, None]
        attn_xal = {}
        prev_rs = [0]
        rs_tok = {}
        o3_done_tok = {}
        o3_half1_tok = {}
        yhi_latest = [0]
        pl_chain = [0]
        pending = []

        def emit_xa_load(m, i):
            kt = GRP * i + m
            msl = m % 2
            wait_free("pool", xa_free[msl])
            dma("pool", s_xa[msl], xa[msl][0:64, :], xin[0, 64 * kt:64 * kt + 64, :])
            attn_xal[(i, m)] = (s_xa[msl], dma("pool", s_xa[msl], xa[msl][64:128, :],
                                               xin[1, 64 * kt:64 * kt + 64, :]))

        def emit_attn(i, m, pb4=4, pb5=5):
            wait("te", s_sth, 2 * (256 * i + 64 * (m + 1)))
            bank_wait("te", pb4)
            bank_wait("te", pb5)
            s_te.tok()
            emit("te", lambda e, i=i, m=m, pb4=pb4: e.matmul(
                P[pb4][:, :], statesT[:, 512 * (i % 2) + 128 * m:512 * (i % 2) + 128 * (m + 1)],
                ctb[:, 0:512], start=True, stop=True).then_inc(s_te.h, 1))
            t_at0 = s_te.n
            s_te.tok()
            emit("te", lambda e, i=i, m=m, pb5=pb5: e.matmul(
                P[pb5][:, :], statesT[:, 512 * (i % 2) + 128 * m:512 * (i % 2) + 128 * (m + 1)],
                ctb[:, 512:1024], start=True, stop=True).then_inc(s_te.h, 1))
            t_at1 = s_te.n
            if m == GRP - 1:
                attn_last_tok[i] = t_at1
            t_ae0 = dve_step(lambda e, m=m, pb4=pb4: e.tensor_copy(
                attn_sb[m][:, 0:512], P[pb4][:, :]), waits=[(s_te, t_at0)])
            bank_set_free(pb4, s_dve, t_ae0)
            t_ae1 = dve_step(lambda e, m=m, pb5=pb5: e.tensor_copy(
                attn_sb[m][:, 512:1024], P[pb5][:, :]), waits=[(s_te, t_at1)])
            bank_set_free(pb5, s_dve, t_ae1)

        def emit_p1(i, m):
            kt = GRP * i + m
            msl = m % 2
            t_xnf = dve_step(lambda e, kt=kt, msl=msl: e.tensor_scalar(
                cbA[:, :], xa[msl][:, :], mu_arr[:, kt:kt + 1], r_arr[:, kt:kt + 1],
                op0=ALU.subtract, op1=ALU.mult),
                waits=[attn_xal[(i, m)]])
            wait("dve", s_dve, t_xnf)
            t_t1 = dve_step(lambda e: e.tensor_tensor(
                cbB[:, :], cbA[:, :], d1B[:, :], op=ALU.mult))
            wait("dve", s_dve, t_t1)
            t_yhi = dve_step(lambda e, m=m: e.tensor_tensor(
                cbC[:, :], cbB[:, :], attn_sb[m][:, :], op=ALU.add))
            wait("dve", s_dve, t_yhi)
            for fn in ((lambda e, msl=msl: e.bn_stats(bn6C[msl][:, 0:6], cbC[:, 0:512])),
                       (lambda e, msl=msl: e.bn_stats(bn6C[msl][:, 6:12], cbC[:, 512:1024])),
                       (lambda e, msl=msl: e.bn_aggr(mvC[msl][:, :], bn6C[msl][:, :]))):
                tok = dve_step(fn)
                wait("dve", s_dve, tok)
            rs_tok[(i, m)] = newton_rsqrt(mvC[msl][:, 1:2], r2r[msl][:, :], msl,
                                          wset=(nwC[msl], nwiC[msl]))

        def emit_p2(i, m):
            kt = GRP * i + m
            msl = m % 2
            wait("dve", s_dve, rs_tok[(i, m)])
            t_t2 = dve_step(lambda e, msl=msl: e.tensor_scalar(
                cbA[:, :], cbC[:, :], mvC[msl][:, 0:1], r2r[msl][:, :],
                op0=ALU.subtract, op1=ALU.mult))
            wait("dve", s_dve, t_t2)
            t_t3 = dve_step(lambda e: e.tensor_tensor(
                cbB[:, :], cbA[:, :], gammaB[:, :], op=ALU.mult))
            wait("dve", s_dve, t_t3)
            t_t4 = dve_step(lambda e: e.tensor_tensor(
                cbC[:, :], cbB[:, :], betaB[:, :], op=ALU.add))
            wait("dve", s_dve, t_t4)
            t_o1 = dve_step(lambda e, msl=msl: e.tensor_tensor(
                cbA[:, :], cbC[:, :], xa[msl][:, :], op=ALU.add))
            wait("dve", s_dve, t_o1)
            xa_free[msl] = (s_dve, t_o1)
            w_ = [outs_free[msl]] if outs_free[msl] is not None else []
            t_out = dve_step(lambda e, msl=msl, m=m: e.tensor_tensor(
                outs[msl][:, :], cbA[:, :], o3_sb[:, 1024 * m:1024 * (m + 1)],
                op=ALU.add), waits=w_)
            if m == GRP - 1:
                o3_freeL[0] = (s_dve, t_out)
            wait("pool", s_dve, t_out)
            dma("pool", s_st[msl], out_d[0, 64 * kt:64 * kt + 64, :], outs[msl][0:64, :])
            t_st = dma("pool", s_st[msl], out_d[1, 64 * kt:64 * kt + 64, :], outs[msl][64:128, :])
            outs_free[msl] = (s_st[msl], t_st)

        def pop_one():
            kind, i_c, m_c = pending.pop(0)
            if kind == "attn":
                emit_attn(i_c, m_c)
            elif kind == "p1":
                emit_p1(i_c, m_c)
            else:
                emit_p2(i_c, m_c)
                if m_c + 2 < GRP:
                    emit_xa_load(m_c + 2, i_c)

        def emit_prologue_te(kt):
            for c in range(4):
                emit_prologue_te_chunk(kt, c)

        t_cl = emit_sp_consts()
        wait("dve", s_cl, t_cl)
        tok_d1b = s_dve.tok()
        emit("dve", lambda e: e.tensor_scalar(d1B[:, :], cbA[:, :], 1.0, None, op0=ALU.add).then_inc(s_dve.h, 1))
        wait("sp", s_cl, t_cl)
        for kt in range(GRP):
            emit_prologue_load(kt)
            emit_prologue_te(kt)
        emit_prologue_load(GRP + 0)
        emit_prologue_load(GRP + 1)

        W12ROT = [0, 1, 2, 5]
        POP_SLOTS = (0, 1, 2, 3, 6, 8, 12, 14, 18, 20, 24, 26)

        for i in range(NFT):
            isl = i % 2
            last_tile = (i == NFT - 1)
            pro_loads = {7: 2, 13: 3}
            pro_tes = {3: (0, 0), 4: (0, 1), 5: (0, 2), 6: (0, 3),
                       9: (1, 0), 10: (1, 1), 11: (1, 2), 12: (1, 3),
                       15: (2, 0), 16: (2, 1), 17: (2, 2), 18: (2, 3),
                       21: (3, 0), 22: (3, 1), 23: (3, 2), 24: (3, 3)}
            for j in range(32):
                if j in POP_SLOTS and pending:
                    pop_one()
                if j % 2 == 0:
                    sw12, t_w12 = load_weight(None, 4096, ("w12", j // 2))
                off = 2048 * (j % 2)
                w1ap = wsl[sw12][:, off:off + 1024]
                w2ap = wsl[sw12][:, off + 1024:off + 2048]
                b1 = W12ROT[(2 * j) % 4]
                b2 = W12ROT[(2 * j + 1) % 4]
                if j == 0:
                    wait("te", s_act, group_xnT_tok[i])
                if j % 2 == 0:
                    wait("te", t_w12[0], t_w12[1])
                bank_wait("te", b1)
                for k in range(8):
                    last = (k == 7)
                    if last:
                        s_te.tok()
                    def mk(k=k, b1=b1, w1ap=w1ap, isl=isl, last=last):
                        def f(e):
                            ins = e.matmul(P[b1][:, :], w1ap[:, 128 * k:128 * (k + 1)],
                                           xnT[isl][:, 512 * k:512 * (k + 1)],
                                           start=(k == 0), stop=last)
                            if last:
                                ins.then_inc(s_te.h, 1)
                            return ins
                        return f
                    fmm(mk())
                t_g1 = s_te.n
                bank_wait("te", b2)
                for k in range(8):
                    last = (k == 7)
                    if last:
                        s_te.tok()
                    def mk(k=k, b2=b2, w2ap=w2ap, isl=isl, last=last):
                        def f(e):
                            ins = e.matmul(P[b2][:, :], w2ap[:, 128 * k:128 * (k + 1)],
                                           xnT[isl][:, 512 * k:512 * (k + 1)],
                                           start=(k == 0), stop=last)
                            if last:
                                ins.then_inc(s_te.h, 1)
                            return ins
                        return f
                    fmm(mk())
                t_g2 = s_te.n
                if j % 2 == 1:
                    set_slot_free(sw12, t_g2)
                wait("act", s_te, t_g1)
                wait_free("act", u_free[j % 2])
                t_u = s_act.tok()
                emit("act", lambda e, j=j, b1=b1: e.activation(
                    u_sb[j % 2][:, :], P[b1][:, :], AF.Silu,
                    scale=s1c[:, j:j + 1]).then_inc(s_act.h, 1))
                bank_set_free(b1, s_act, t_u)
                wait("act", s_te, t_g2)
                t_v0 = s_act.tok()
                emit("act", lambda e, j=j, b2=b2: e.activation(
                    v_sb[j % 2][:, :], P[b2][:, :], AF.Identity,
                    scale=s2c[:, j:j + 1]).then_inc(s_act.h, 1))
                bank_set_free(b2, s_act, t_v0)
                wait("dve", s_act, t_v0)
                t_a = s_dve.tok()
                emit("dve", lambda e, j=j: e.tensor_tensor(
                    a_sb[:, 512 * j:512 * (j + 1)], u_sb[j % 2][:, :],
                    v_sb[j % 2][:, :], op=ALU.mult).then_inc(s_dve.h, 1))
                u_free[j % 2] = (s_dve, t_a)
                if j == 31:
                    a_done_tok = t_a
                if i + 1 < NFT:
                    if j in pro_loads:
                        emit_prologue_load(GRP * (i + 1) + pro_loads[j])
                    if j in pro_tes:
                        kt_l, c_l = pro_tes[j]
                        emit_prologue_te_chunk(GRP * (i + 1) + kt_l, c_l)
            xnT_free[isl] = (s_te, t_g2)

            if DEBUG and i == 0:
                wait("sp", s_dve, a_done_tok)
                t_dbg = dma("sp", s_st[0], dbg_a[:, :], a_sb[:, :])
                wait("te", s_st[0], t_dbg)

            for j in range(32):
                swh, t_wh = load_weight(None, 4096, ("h1", j))
                bk = j % 2
                wait("te", t_wh[0], t_wh[1])
                bank_wait("te", bk)
                if j == 0:
                    wait("te", s_dve, a_done_tok)
                for k in range(32):
                    last = (k == 31)
                    if last:
                        s_te.tok()
                    def mk(k=k, bk=bk, swh=swh, last=last):
                        def f(e):
                            ins = e.matmul(P[bk][:, :], wsl[swh][:, 128 * k:128 * (k + 1)],
                                           a_sb[:, 512 * k:512 * (k + 1)],
                                           start=(k == 0), stop=last)
                            if last:
                                ins.then_inc(s_te.h, 1)
                            return ins
                        return f
                    fmm(mk())
                t_g = s_te.n
                set_slot_free(swh, t_g)
                wait("act", s_te, t_g)
                t_gelu = s_act.tok()
                emit("act", lambda e, j=j, bk=bk: e.activation(
                    g1_sb[:, 512 * j:512 * (j + 1)], P[bk][:, :], AF.Gelu,
                    scale=h1sc[:, j:j + 1]).then_inc(s_act.h, 1))
                bank_set_free(bk, s_act, t_gelu)
                if j == 8 and i + 2 < NFT:
                    emit_prologue_load(GRP * (i + 2) + 0)
            h1_gelu_done = t_gelu

            if DEBUG and i == 0:
                wait("sp", s_act, h1_gelu_done)
                t_dbg = dma("sp", s_st[0], dbg_g1[:, :], g1_sb[:, :])
                wait("te", s_st[0], t_dbg)

            for j in range(32):
                swh, t_wh = load_weight(None, 4096, ("h2", j))
                bk = j % 2
                wait("te", t_wh[0], t_wh[1])
                bank_wait("te", bk)
                if j == 0:
                    wait("te", s_act, h1_gelu_done)
                for k in range(32):
                    last = (k == 31)
                    if last:
                        s_te.tok()
                    def mk(k=k, bk=bk, swh=swh, last=last):
                        def f(e):
                            ins = e.matmul(P[bk][:, :], wsl[swh][:, 128 * k:128 * (k + 1)],
                                           g1_sb[:, 512 * k:512 * (k + 1)],
                                           start=(k == 0), stop=last)
                            if last:
                                ins.then_inc(s_te.h, 1)
                            return ins
                        return f
                    fmm(mk())
                t_g = s_te.n
                set_slot_free(swh, t_g)
                wait("act", s_te, t_g)
                t_gelu = s_act.tok()
                emit("act", lambda e, j=j, bk=bk: e.activation(
                    a_sb[:, 512 * j:512 * (j + 1)], P[bk][:, :], AF.Gelu,
                    scale=h2sc[:, j:j + 1]).then_inc(s_act.h, 1))
                bank_set_free(bk, s_act, t_gelu)
                if j == 8 and i + 2 < NFT:
                    emit_prologue_load(GRP * (i + 2) + 1)
            g2_done_tok = t_gelu

            if DEBUG and i == 0:
                wait("sp", s_act, g2_done_tok)
                t_dbg = dma("sp", s_st[0], dbg_g2[:, :], a_sb[:, :])
                wait("te", s_st[0], t_dbg)

            if last_tile:
                emit_xa_load(0, i)
                emit_xa_load(1, i)

            W3B = [0, 1, 2, 5]
            for half in range(2):
                for m in range(4):
                    bank_wait("te", W3B[m])
                for kg in range(4):
                    sw3, t_w3 = load_weight(None, 4096, ("w3", half, kg))
                    wait("te", t_w3[0], t_w3[1])
                    if half == 0 and kg == 0:
                        wait("te", s_act, g2_done_tok)
                    for kl in range(8):
                        k = 8 * kg + kl
                        for m in range(4):
                            last = (kl == 7 and m == 3)
                            stop = (kg == 3 and kl == 7)
                            if last:
                                s_te.tok()
                            def mk(k=k, kl=kl, m=m, sw3=sw3, last=last, stop=stop,
                                   first=(k == 0)):
                                def f(e):
                                    ins = e.matmul(
                                        P[W3B[m]][:, :],
                                        a_sb[:, 512 * k + 128 * m:512 * k + 128 * (m + 1)],
                                        wsl[sw3][:, 512 * kl:512 * (kl + 1)],
                                        start=first, stop=stop)
                                    if last:
                                        ins.then_inc(s_te.h, 1)
                                    return ins
                                return f
                            fmm(mk())
                    set_slot_free(sw3, s_te.n)
                t_w3h = s_te.n
                for m in range(4):
                    wait("dve", s_te, t_w3h)
                    if half == 0 and m == 0 and o3_freeL[0] is not None:
                        wait_free("dve", o3_freeL[0])
                    t_e = s_dve.tok()
                    emit("dve", lambda e, m=m, half=half: e.tensor_tensor(
                        o3_sb[:, 1024 * m + 512 * half:1024 * m + 512 * (half + 1)],
                        P[W3B[m]][:, :], s3B[:, 512 * half:512 * (half + 1)],
                        op=ALU.mult).then_inc(s_dve.h, 1))
                    bank_set_free(W3B[m], s_dve, t_e)

            if last_tile:
                drain_scan(SEQ)
                for m in range(GRP):
                    emit_attn(i, m)
                for m in range(GRP):
                    emit_p1(i, m)
                    emit_p2(i, m)
                    if m + 2 < GRP:
                        emit_xa_load(m + 2, i)
            else:
                emit_xa_load(0, i)
                emit_xa_load(1, i)
                for m in range(GRP):
                    pending.append(("attn", i, m))
                for m in range(GRP):
                    pending.append(("p1", i, m))
                    pending.append(("p2", i, m))

        assert not pending, f"{len(pending)} entries left in combine queue"

        if DEBUG:
            wait("sp", s_sth, 2 * SEQ)
            dma("sp", s_st[0], dbg_mu[:, :], mu_arr[:, :])
            dma("sp", s_st[0], dbg_r[:, :], r_arr[:, :])

        emit("pool", lambda e: e.wait_ge(s_st[0].h, s_st[0].n))
        emit("pool", lambda e: e.wait_ge(s_st[1].h, s_st[1].n))

        block = stack.enter_context(nc.Block())

        @block.sync
        def _(sync):
            for f in E["sp"]:
                f(sync)

        @block.gpsimd
        def _(gpsimd):
            for f in E["pool"]:
                f(gpsimd)

        @block.tensor
        def _(tensor):
            for f in E["te"]:
                f(tensor)

        @block.scalar
        def _(scalar):
            for f in E["act"]:
                f(scalar)

        @block.vector
        def _(vector):
            for f in E["dve"]:
                f(vector)

    return nc


def _prep_shared(L0_attn_A, L0_attn_B, L0_attn_C, L0_attn_D, L0_attn_gamma,
                 L0_attn_beta, L0_ffn_w1_w, L0_ffn_w1_s, L0_ffn_w2_w, L0_ffn_w2_s,
                 L0_ffn_h1_w, L0_ffn_h1_s, L0_ffn_h2_w, L0_ffn_h2_s,
                 L0_ffn_w3_w, L0_ffn_w3_s):
    c = np.ascontiguousarray
    f32 = np.float32
    def tile_w12(w):
        return c(w.reshape(32, 128, 8, 128).transpose(0, 3, 2, 1).reshape(32, 128, 8 * 128)).astype(f32)
    def tile_h(w):
        return c(w.reshape(32, 128, 32, 128).transpose(0, 3, 2, 1).reshape(32, 128, 32 * 128)).astype(f32)
    def tile_w3(w):
        return c(w.reshape(2, 512, 4, 8, 128).transpose(0, 2, 4, 3, 1).reshape(2, 4, 128, 8 * 512)).astype(f32)
    w1t = tile_w12(np.asarray(L0_ffn_w1_w))
    w2t = tile_w12(np.asarray(L0_ffn_w2_w))
    w12 = np.empty((16, 128, 4096), np.float32)
    w12[:, :, 0:1024] = w1t[0::2]
    w12[:, :, 1024:2048] = w2t[0::2]
    w12[:, :, 2048:3072] = w1t[1::2]
    w12[:, :, 3072:4096] = w2t[1::2]
    return {
        "w12p": np.ascontiguousarray(w12),
        "h1p": tile_h(np.asarray(L0_ffn_h1_w)),
        "h2p": tile_h(np.asarray(L0_ffn_h2_w)),
        "w3p": tile_w3(np.asarray(L0_ffn_w3_w)),
        "btp": c(np.asarray(L0_attn_B).T.reshape(8, 128, 128).transpose(1, 0, 2).reshape(128, 8 * 128)).astype(f32),
        "ctp": c(np.asarray(L0_attn_C).T).astype(f32),
        "atp": c(np.asarray(L0_attn_A).T).astype(f32),
        "identp": np.eye(128, dtype=bfloat16),
        "gammap": c(np.broadcast_to(np.asarray(L0_attn_gamma), (128, D))).astype(f32),
        "betap": c(np.broadcast_to(np.asarray(L0_attn_beta), (128, D))).astype(f32),
        "dp": c(np.broadcast_to(np.asarray(L0_attn_D), (128, D))).astype(f32),
        "s3p": c(np.broadcast_to(np.asarray(L0_ffn_w3_s), (128, D))).astype(f32),
        "s1p": c(np.asarray(L0_ffn_w1_s).reshape(32, 128).T).astype(f32),
        "s2p": c(np.asarray(L0_ffn_w2_s).reshape(32, 128).T).astype(f32),
        "h1sp": c(np.asarray(L0_ffn_h1_s).reshape(32, 128).T).astype(f32),
        "h2sp": c(np.asarray(L0_ffn_h2_s).reshape(32, 128).T).astype(f32),
    }


def kernel(L0_attn_x, L0_attn_A, L0_attn_B, L0_attn_C, L0_attn_D, L0_attn_gamma,
           L0_attn_beta, L0_ffn_w1_w, L0_ffn_w1_s, L0_ffn_w2_w, L0_ffn_w2_s,
           L0_ffn_h1_w, L0_ffn_h1_s, L0_ffn_h2_w, L0_ffn_h2_s,
           L0_ffn_w3_w, L0_ffn_w3_s):
    x = np.ascontiguousarray(np.asarray(L0_attn_x, dtype=np.float32))
    shared = _prep_shared(L0_attn_A, L0_attn_B, L0_attn_C, L0_attn_D, L0_attn_gamma,
                          L0_attn_beta, L0_ffn_w1_w, L0_ffn_w1_s, L0_ffn_w2_w,
                          L0_ffn_w2_s, L0_ffn_h1_w, L0_ffn_h1_s, L0_ffn_h2_w,
                          L0_ffn_h2_s, L0_ffn_w3_w, L0_ffn_w3_s)
    nc = build_program()
    in_maps = []
    for c in range(8):
        m = dict(shared)
        m["xin"] = np.ascontiguousarray(x[2 * c:2 * c + 2])
        in_maps.append(m)
    res = run_bass_kernel_spmd(nc, in_maps, core_ids=list(range(8)))
    out = np.empty((16, SEQ, D), np.float32)
    for c in range(8):
        out[2 * c:2 * c + 2] = res.results[c]["out"]
    return out
